# version 01
# baseline (speedup 1.0000x reference)
import numpy as np
import ml_dtypes
import concourse.bass as bass
import concourse.mybir as mybir
from concourse.bass_utils import run_bass_kernel_spmd

F32 = mybir.dt.float32
BF16 = mybir.dt.bfloat16
AF = mybir.ActivationFunctionType
ALU = mybir.AluOpType
AX = mybir.AxisListType

D = 1024
DH = 4096
H = 16
HD = 64
S = 2048
B_CORE = 2
TT = 512
BS = 64
M_BLK = S // BS
SCALE = HD ** -0.5
LN_EPS = 1e-5
ATTB_EPS = 1e-6
BIG = 2.0 ** 100

_DEBUG_STOP_AFTER = None
_DEBUG_EXPOSE = False
_DEBUG_TAGS = {}
_DBG_WORDS = 65536


def ts(i, n):
    return slice(i * n, (i + 1) * n)


class Sem:
    def __init__(self, nc, name):
        self.h = nc.alloc_semaphore(name=name)
        self.n = 0


class Prog:
    ENGINES = ("tensor", "vector", "scalar", "gpsimd", "sync")

    def __init__(self, nc):
        self.nc = nc
        self.steps = {e: [] for e in self.ENGINES}
        self.res = {}
        self.sems = {}
        self.bar = Sem(nc, "barrier")

    def sem(self, name):
        if name not in self.sems:
            self.sems[name] = Sem(self.nc, name)
        return self.sems[name]

    def on(self, eng, fn):
        self.steps[eng].append(fn)

    def wait(self, eng, sem, val):
        if val > 0:
            self.on(eng, lambda e, s=sem.h, v=val: e.wait_ge(s, v))

    def inc(self, eng, sem, k=1):
        fns = self.steps[eng]
        prev = fns[-1]
        def wrapped(e, prev=prev, h=sem.h, k=k):
            ins = prev(e)
            ins.then_inc(h, k)
            return ins
        fns[-1] = wrapped
        sem.n += k
        return sem.n

    def wait_res(self, eng, key):
        ent = self.res.get(key)
        if ent is not None:
            self.wait(eng, ent[0], ent[1])

    def free_res(self, key, sem, val):
        self.res[key] = (sem, val)

    def barrier(self, dma_queues=()):
        for q in dma_queues:
            self.wait(q.eng, q.sem, q.sem.n)
        for eng in self.ENGINES:
            do_drain = eng in ("tensor", "vector", "scalar")
            def _b(e, h=self.bar.h, do_drain=do_drain):
                if do_drain:
                    e.drain()
                ins = e.nop()
                ins.then_inc(h, 1)
                return ins
            self.on(eng, _b)
        self.bar.n += 5
        tgt = self.bar.n
        for eng in self.ENGINES:
            self.wait(eng, self.bar, tgt)
        self.res.clear()

    def emit(self, block):
        for eng in self.ENGINES:
            fns = self.steps[eng]
            def run(e, fns=fns):
                for f in fns:
                    f(e)
            getattr(block, eng)(run)


class DmaQ:
    def __init__(self, prog, eng, name):
        self.p = prog
        self.eng = eng
        self.sem = prog.sem(name)

    def dma(self, dst, src):
        self.p.on(self.eng, lambda e, d=dst, s=src, h=self.sem.h:
                  e.dma_start(out=d, in_=s).then_inc(h, 16))
        self.sem.n += 16
        return self.sem.n


def build_core_program():
    nc = bass.Bass("TRN2", target_bir_lowering=False,
                   detect_race_conditions=False)

    def din(name, shape, dtype=F32):
        return nc.dram_tensor(name, list(shape), dtype, kind="ExternalInput").ap()

    x_in = din("x", (B_CORE, D, S))
    qkv_wT = din("qkv_wT", (D, 3 * D))
    proj_wT = din("proj_wT", (D, D))
    proj_b2 = din("proj_b2", (128, 8))
    wqT = din("wqT", (D, D)); wkT = din("wkT", (D, D))
    wvT = din("wvT", (D, D)); woT = din("woT", (D, D))
    bo2 = din("bo2", (128, 8))
    w1T = din("w1T", (D, DH)); w2T = din("w2T", (D, DH))
    w3T = din("w3T", (DH, D))
    h1T = din("h1T", (DH, DH)); h2T = din("h2T", (DH, DH))
    s1c = din("s1c", (128, 32)); s2c = din("s2c", (128, 32))
    sh1c = din("sh1c", (128, 32)); sh2c = din("sh2c", (128, 32))
    s3c = din("s3c", (128, 8))
    e1c = din("e1c", (128, 128))
    eye128c = din("eye128c", (128, 128))
    ones_c = din("ones_c", (128, 1))
    rkTc = din("rkTc", (M_BLK, S), BF16)
    sel8c = din("sel8c", (8, 512), BF16)
    eye32c = din("eye32c", (8, 4 * M_BLK))
    selhc = din("selhc", (128, D))

    x_out = nc.dram_tensor("out", [B_CORE, D, S], F32, kind="ExternalOutput").ap()

    def dscr(name, shape, dtype=F32):
        kind = "ExternalOutput" if _DEBUG_EXPOSE else "Internal"
        return nc.dram_tensor(name, list(shape), dtype, kind=kind).ap()

    x1 = dscr("x1", (B_CORE, D, S))
    x2 = dscr("x2", (B_CORE, D, S))
    x3 = dscr("x3", (B_CORE, D, S))
    _otk = "ExternalOutput" if _DEBUG_EXPOSE else "Internal"
    ot_hi = nc.dram_tensor("ot_hi", [8, 128, S], BF16, kind=_otk).ap()
    ot_lo = nc.dram_tensor("ot_lo", [8, 128, S], BF16, kind=_otk).ap()
    dbg = (nc.dram_tensor("dbg", [128, _DBG_WORDS], F32, kind="ExternalOutput").ap()
           if _DEBUG_EXPOSE else None)
    _dbg_cur = [0]
    c_w1 = nc.dram_tensor("c_w1", [32, 128, 8, 128], BF16, kind="Internal").ap()
    c_w2 = nc.dram_tensor("c_w2", [32, 128, 8, 128], BF16, kind="Internal").ap()
    c_w3 = nc.dram_tensor("c_w3", [8, 128, 32, 128], BF16, kind="Internal").ap()
    c_h1 = nc.dram_tensor("c_h1", [32, 128, 32, 128], BF16, kind="Internal").ap()
    c_h2 = nc.dram_tensor("c_h2", [32, 128, 32, 128], BF16, kind="Internal").ap()

    p = Prog(nc)
    qW = DmaQ(p, "sync", "qW")
    qX = DmaQ(p, "gpsimd", "qX")
    sMM = p.sem("sMM")
    sEV = p.sem("sEV")
    sVE = p.sem("sVE")
    sSC = p.sem("sSC")

    def dbg_dump(tag, ap, pdim, words):
        if dbg is None or tag in _DEBUG_TAGS:
            return
        off = _dbg_cur[0]
        assert off + words <= _DBG_WORDS
        _DEBUG_TAGS[tag] = (off, [pdim, words])
        _dbg_cur[0] = off + words
        v = qX.dma(dbg[0:pdim, off:off + words], ap)
        for _e in ("vector", "scalar", "tensor"):
            p.wait(_e, qX.sem, v)

    def T(fn): p.on("tensor", fn)
    def V(fn): p.on("vector", fn)
    def SC(fn): p.on("scalar", fn)

    def sb(name, pdim, fdim, dtype=F32):
        return nc.alloc_sbuf_tensor(name, [pdim, fdim], dtype).ap()

    cst_e1 = sb("cst_e1", 128, 128)
    cst_eye = sb("cst_eye", 128, 128)
    cst_ones = sb("cst_ones", 128, 1)
    cst_rkT = sb("cst_rkT", M_BLK, S, BF16)
    cst_sel8 = sb("cst_sel8", 8, 512, BF16)
    cst_eye32 = sb("cst_eye32", 8, 4 * M_BLK)
    cst_selh = sb("cst_selh", 128, D)
    sc_s1 = sb("sc_s1", 128, 32)
    sc_s2 = sb("sc_s2", 128, 32)
    sc_sh1 = sb("sc_sh1", 128, 32)
    sc_sh2 = sb("sc_sh2", 128, 32)
    sc_s3 = sb("sc_s3", 128, 8)
    sc_pb = sb("sc_pb", 128, 8)
    sc_bo = sb("sc_bo", 128, 8)
    stage = sb("stage", 128, 2 * TT)
    x_sb = sb("x_sb", 128, 8 * TT)
    wbufA = sb("wbufA", 128, 4096, BF16)
    wbufB = sb("wbufB", 128, 4096, BF16)
    tmp1 = sb("tmp1", 128, TT)
    tmp2 = sb("tmp2", 128, TT)
    tmp3 = sb("tmp3", 128, TT)
    row1 = sb("row1", 1, TT)
    row2 = sb("row2", 1, TT)
    colA = sb("colA", 128, 8)
    colC = sb("colC", 128, 32)
    statcol = sb("statcol", 128, 64)

    ARENA_W = 37120
    arena = sb("arena", 128, ARENA_W)

    class Arena:
        def __init__(self):
            self.off = 0
        def alloc(self, words):
            a = self.off
            self.off += words
            assert self.off <= ARENA_W, f"arena overflow {self.off}"
            return arena[:, a:a + words]
        def allocb(self, belems):
            return self.alloc((belems + 1) // 2).bitcast(BF16)
        def view(self, off_words, words):
            return arena[:, off_words:off_words + words]

    def pst(name):
        return nc.alloc_psum_tensor(name, [128, 512], F32).ap()

    psA = pst("psA"); psB = pst("psB"); psC = pst("psC"); psD = pst("psD")
    psE = pst("psE"); psF = pst("psF"); psG = pst("psG"); psH = pst("psH")
    PS = {"psA": psA, "psB": psB, "psC": psC, "psD": psD,
          "psE": psE, "psF": psF, "psG": psG, "psH": psH}

    for dst, src in [(cst_e1[:], e1c), (cst_eye[:], eye128c),
                     (cst_ones[:], ones_c), (cst_rkT[:], rkTc),
                     (cst_sel8[:], sel8c), (cst_eye32[:], eye32c),
                     (cst_selh[:], selhc),
                     (sc_s1[:], s1c), (sc_s2[:], s2c), (sc_sh1[:], sh1c),
                     (sc_sh2[:], sh2c), (sc_s3[:], s3c), (sc_pb[:], proj_b2),
                     (sc_bo[:], bo2)]:
        qX.dma(dst, src)
    p.wait("vector", qX.sem, qX.sem.n)
    V(lambda e: e.memset(stage[:], 0.0))
    V(lambda e: e.memset(colA[:], 0.0))
    V(lambda e: e.memset(colC[:], 0.0))
    p.barrier(dma_queues=(qX, qW))

    def cache_build():
        specs = [(w1T, c_w1, 8, 32), (w2T, c_w2, 8, 32), (w3T, c_w3, 32, 8),
                 (h1T, c_h1, 32, 32), (h2T, c_h2, 32, 32)]
        bufs = [wbufA, wbufB]
        store_done = [0, 0]
        seq_i = 0
        for src, dst, n_ich, n_oc in specs:
            o_full = n_oc * 128
            for ich in range(n_ich):
                bi = seq_i % 2
                buf = bufs[bi]
                p.wait("gpsimd", qW.sem, store_done[bi])
                v_ld = qX.dma(buf[:, 0:o_full], src[ts(ich, 128), :])
                p.wait("sync", qX.sem, v_ld)
                last = 0
                for oc in range(n_oc):
                    last = qW.dma(dst[oc, :, ich, :], buf[:, ts(oc, 128)])
                store_done[bi] = last
                seq_i += 1
        p.barrier(dma_queues=(qX, qW))

    cache_build()

    def load_x_tile(src, seq, t0):
        src_ap = src[seq, :, t0:t0 + TT].rearrange("(c p) t -> p c t", p=128)
        p.wait_res("gpsimd", "x_sb")
        return qX.dma(x_sb[:, :].rearrange("p (c t) -> p c t", c=8), src_ap)

    def store_x_tile(dst, seq, t0, src_sb=None, key="x_sb"):
        src_sb = x_sb if src_sb is None else src_sb
        dst_ap = dst[seq, :, t0:t0 + TT].rearrange("(c p) t -> p c t", p=128)
        v = qX.dma(dst_ap, src_sb[:, 0:8 * TT].rearrange("p (c t) -> p c t",
                                                         c=8))
        p.free_res(key, qX.sem, v)
        return v

    def ln_tile(vx_ld):
        p.wait("tensor", qX.sem, vx_ld)
        p.wait("scalar", qX.sem, vx_ld)
        p.wait_res("tensor", "psG")
        p.wait_res("tensor", "psH")
        for i in range(8):
            T(lambda e, i=i: e.matmul(psG[0:1, 0:TT], cst_ones[:, :],
                                      x_sb[:, ts(i, TT)],
                                      start=(i == 0), stop=(i == 7)))
        p.wait_res("scalar", "tmp1")
        p.wait_res("scalar", "tmp2")
        for i in range(8):
            dst = tmp1 if i % 2 == 0 else tmp2
            SC(lambda e, i=i, dst=dst:
               e.activation(dst[:, 0:TT], x_sb[:, ts(i, TT)], AF.Square))
            vsc = p.inc("scalar", sSC)
            p.wait("tensor", sSC, vsc)
            T(lambda e, dst=dst, i=i: e.matmul(psH[0:1, 0:TT], cst_ones[:, :],
                                               dst[:, 0:TT],
                                               start=(i == 0), stop=(i == 7)))
            if i < 7:
                vmm = p.inc("tensor", sMM)
                p.wait("scalar", sMM, vmm)
        vmm = p.inc("tensor", sMM)
        p.wait("vector", sMM, vmm)
        cinv = 1.0 / D
        def stats(e):
            e.tensor_scalar_mul(stage[0:1, 0:TT], psG[0:1, 0:TT], cinv)
            e.tensor_scalar_mul(row1[0:1, :], psH[0:1, 0:TT], cinv)
            e.tensor_tensor(row2[0:1, :], stage[0:1, 0:TT], stage[0:1, 0:TT],
                            ALU.mult)
            e.tensor_sub(row1[0:1, :], row1[0:1, :], row2[0:1, :])
            e.tensor_scalar_add(row1[0:1, :], row1[0:1, :], LN_EPS)
            return e.reciprocal(row2[0:1, :], row1[0:1, :])
        V(stats)
        vv = p.inc("vector", sVE)
        p.wait("scalar", sVE, vv)
        SC(lambda e: e.activation(stage[0:1, TT:2 * TT], row2[0:1, :], AF.Sqrt))
        vsc = p.inc("scalar", sSC)
        p.wait("tensor", sSC, vsc)
        T(lambda e: e.matmul(psG[:, 0:TT], cst_e1[:, :], stage[:, 0:TT],
                             start=True, stop=True))
        T(lambda e: e.matmul(psH[:, 0:TT], cst_e1[:, :], stage[:, TT:2 * TT],
                             start=True, stop=True))
        return p.inc("tensor", sMM)

    def normalize_f32(vln, out_ap):
        p.wait("vector", sMM, vln)
        for i in range(8):
            def f(e, i=i):
                e.tensor_sub(out_ap[:, ts(i, TT)], x_sb[:, ts(i, TT)],
                             psG[:, 0:TT])
                return e.tensor_mul(out_ap[:, ts(i, TT)], out_ap[:, ts(i, TT)],
                                    psH[:, 0:TT])
            V(f)
        vv = p.inc("vector", sVE)
        p.free_res("psG", sVE, vv)
        p.free_res("psH", sVE, vv)
        return vv

    def normalize_split(vln, hi_ap, lo_ap):
        p.wait("vector", sMM, vln)
        p.wait_res("vector", "tmp3")
        for i in range(8):
            def f(e, i=i):
                e.tensor_sub(tmp3[:, 0:TT], x_sb[:, ts(i, TT)], psG[:, 0:TT])
                e.tensor_mul(tmp3[:, 0:TT], tmp3[:, 0:TT], psH[:, 0:TT])
                e.tensor_copy(hi_ap[:, ts(i, TT)], tmp3[:, 0:TT])
                return e.tensor_tensor(lo_ap[:, ts(i, TT)], tmp3[:, 0:TT],
                                       hi_ap[:, ts(i, TT)], ALU.subtract)
            V(f)
        vv = p.inc("vector", sVE)
        p.free_res("psG", sVE, vv)
        p.free_res("psH", sVE, vv)
        p.free_res("tmp3", sVE, vv)
        return vv

    def split_weightT(dram_wT, hi_ap, lo_ap, n_i=8, o_full=D):
        vv = 0
        for i in range(n_i):
            p.wait_res("gpsimd", "x_sb")
            v = qX.dma(x_sb[:, 0:o_full], dram_wT[ts(i, 128), :])
            p.wait("vector", qX.sem, v)
            def f(e, i=i, o_full=o_full):
                e.tensor_copy(hi_ap[:, i * o_full:(i + 1) * o_full],
                              x_sb[:, 0:o_full])
                return e.tensor_tensor(lo_ap[:, i * o_full:(i + 1) * o_full],
                                       x_sb[:, 0:o_full],
                                       hi_ap[:, i * o_full:(i + 1) * o_full],
                                       ALU.subtract)
            V(f)
            vv = p.inc("vector", sVE)
            p.free_res("x_sb", sVE, vv)
        return vv

    def mm3(ps_ap, w_hi, w_lo, a_hi, a_lo, start, stop, tp=None):
        T(lambda e: e.matmul(ps_ap, w_hi, a_hi, start=start, stop=False,
                             tile_position=tp))
        T(lambda e: e.matmul(ps_ap, w_hi, a_lo, start=False, stop=False,
                             tile_position=tp))
        T(lambda e: e.matmul(ps_ap, w_lo, a_hi, start=False, stop=stop,
                             tile_position=tp))

    def mm2(ps_ap, w, a_hi, a_lo, start, stop):
        T(lambda e: e.matmul(ps_ap, w, a_hi, start=start, stop=False))
        T(lambda e: e.matmul(ps_ap, w, a_lo, start=False, stop=stop))

    def attn_a():
        AR = Arena()
        xn = AR.alloc(8 * TT)
        pnorm = arena[:, 0:S]
        khi = AR.allocb(4 * S)
        klo = AR.allocb(4 * S)
        vtm = AR.alloc(16 * TT)
        qhi = AR.allocb(4 * TT)
        qlo = AR.allocb(4 * TT)
        pbuf = AR.alloc(S)
        ptb = AR.alloc(S)
        kb_sb = AR.alloc(4 * M_BLK)
        qb_sb = AR.alloc(4 * 8)
        qpad_off = AR.off
        qpad_all = AR.alloc(64)
        kpad_all = AR.alloc(8 * M_BLK)
        c_all = AR.alloc(8 * M_BLK)
        A_all = AR.alloc(8 * M_BLK)
        ism = arena[:, qpad_off:qpad_off + 8 * M_BLK]
        m1a = AR.alloc(8)
        m2a = AR.alloc(8)
        mskT = AR.allocb(8)
        t1sb = AR.allocb(S)
        wslab = AR.alloc(4096)
        wslab2_off = AR.off
        wslab2 = AR.alloc(4096)
        stghi = arena[:, wslab2_off:wslab2_off + 256].bitcast(BF16)
        stglo = arena[:, wslab2_off + 256:wslab2_off + 512].bitcast(BF16)
        proj_off = 0

        for seq in range(B_CORE):
            for hg in range(2):
                kcol = D + hg * 512
                vcol = 2 * D + hg * 512
                qcol = hg * 512
                p.wait_res("sync", "wslab")
                p.wait_res("sync", "stg")
                qW.dma(wslab[:, :].rearrange("p (c o) -> p c o", c=8),
                       qkv_wT[:, kcol:kcol + 512].rearrange(
                           "(c p) o -> p c o", p=128))
                vW = qW.dma(wslab2[:, :].rearrange("p (c o) -> p c o", c=8),
                            qkv_wT[:, vcol:vcol + 512].rearrange(
                                "(c p) o -> p c o", p=128))
                p.wait("tensor", qW.sem, vW)

                for tt in range(4):
                    vx = load_x_tile(x_in, seq, tt * TT)
                    vln = ln_tile(vx)
                    vn = normalize_f32(vln, xn)
                    p.wait("tensor", sVE, vn)
                    p.free_res("x_sb", sVE, vn)
                    p.wait("gpsimd", sVE, vn)
                    dbg_dump("xn0", xn[:, 0:TT], 128, TT)
                    for oc in range(4):
                        pk = ["psA", "psB", "psC", "psD"][oc]
                        pt_ = PS[pk]
                        p.wait_res("tensor", pk)
                        for i in range(8):
                            T(lambda e, oc=oc, i=i, pt_=pt_:
                              e.matmul(pt_[:, 0:TT],
                                       wslab[:, i * 512 + oc * 128:
                                             i * 512 + oc * 128 + 128],
                                       xn[:, ts(i, TT)],
                                       start=(i == 0), stop=(i == 7)))
                        vmm = p.inc("tensor", sMM)
                        p.wait("vector", sMM, vmm)
                        def evk(e, oc=oc, tt=tt, pt_=pt_):
                            sl = slice(oc * S + tt * TT, oc * S + tt * TT + TT)
                            e.tensor_reduce(
                                kb_sb[:, oc * M_BLK + tt * 8:
                                      oc * M_BLK + tt * 8 + 8],
                                pt_[:, 0:TT].rearrange("p (n b) -> p n b", b=BS),
                                AX.X, ALU.add)
                            e.tensor_copy(khi[:, sl], pt_[:, 0:TT])
                            return e.tensor_tensor(klo[:, sl], pt_[:, 0:TT],
                                                   khi[:, sl], ALU.subtract)
                        V(evk)
                        vev = p.inc("vector", sEV)
                        p.free_res(pk, sEV, vev)
                        if oc == 0:
                            p.wait("gpsimd", sEV, vev)
                            dbg_dump("khi0", khi[:, 0:TT], 128, TT)
                            dbg_dump("kb0", kb_sb[:, 0:8], 128, 8)
                    for t128 in range(4):
                        pk = "psE" if t128 % 2 == 0 else "psF"
                        pt_ = PS[pk]
                        p.wait_res("tensor", pk)
                        for i in range(8):
                            T(lambda e, t128=t128, i=i, pt_=pt_:
                              e.matmul(pt_[:, 0:TT],
                                       xn[:, i * TT + t128 * 128:
                                          i * TT + t128 * 128 + 128],
                                       wslab2[:, ts(i, TT)],
                                       start=(i == 0), stop=(i == 7)))
                        vmm = p.inc("tensor", sMM)
                        p.wait("vector", sMM, vmm)
                        tc = tt * 4 + t128
                        V(lambda e, tc=tc, pt_=pt_:
                          e.tensor_copy(vtm[:, ts(tc, TT)], pt_[:, 0:TT]))
                        vev = p.inc("vector", sEV)
                        p.free_res(pk, sEV, vev)
                        if tc == 0:
                            p.wait("gpsimd", sEV, vev)
                            dbg_dump("vtm0", vtm[:, 0:TT], 128, TT)
                p.free_res("wslab", sMM, sMM.n)

                p.wait_res("sync", "wslab")
                vq = qW.dma(wslab[:, :].rearrange("p (c o) -> p c o", c=8),
                            qkv_wT[:, qcol:qcol + 512].rearrange(
                                "(c p) o -> p c o", p=128))
                p.wait("tensor", qW.sem, vq)
                for qr in range(4):
                    vx = load_x_tile(x_in, seq, qr * TT)
                    vln = ln_tile(vx)
                    vn = normalize_f32(vln, xn)
                    p.wait("tensor", sVE, vn)
                    p.free_res("x_sb", sVE, vn)
                    for oc in range(4):
                        pk = ["psA", "psB", "psC", "psD"][oc]
                        pt_ = PS[pk]
                        p.wait_res("tensor", pk)
                        for i in range(8):
                            T(lambda e, oc=oc, i=i, pt_=pt_:
                              e.matmul(pt_[:, 0:TT],
                                       wslab[:, i * 512 + oc * 128:
                                             i * 512 + oc * 128 + 128],
                                       xn[:, ts(i, TT)],
                                       start=(i == 0), stop=(i == 7)))
                        vmm = p.inc("tensor", sMM)
                        p.wait("vector", sMM, vmm)
                        def evq(e, oc=oc, pt_=pt_):
                            e.tensor_reduce(
                                qb_sb[:, oc * 8:oc * 8 + 8],
                                pt_[:, 0:TT].rearrange("p (n b) -> p n b", b=BS),
                                AX.X, ALU.add)
                            e.tensor_scalar_mul(qhi[:, ts(oc, TT)],
                                                pt_[:, 0:TT], SCALE)
                            return e.scalar_tensor_tensor(
                                qlo[:, ts(oc, TT)], pt_[:, 0:TT], SCALE,
                                qhi[:, ts(oc, TT)], ALU.mult, ALU.subtract)
                        V(evq)
                        vev = p.inc("vector", sEV)
                        p.free_res(pk, sEV, vev)
                        if oc == 0:
                            p.wait("gpsimd", sEV, vev)
                            dbg_dump("qhi0", qhi[:, 0:TT], 128, TT)
                            dbg_dump("qb0", qb_sb[:, 0:8], 128, 8)
                    p.wait("tensor", sEV, sEV.n)

                    def d3(ap):
                        return ap.rearrange("p (h n) -> p h n", h=8)
                    def padfill(e):
                        e.memset(qpad_all[:], 0.0)
                        e.memset(kpad_all[:], 0.0)
                        last = None
                        for hl in range(8):
                            rr = (hl % 2) * 64
                            ch = hl // 2
                            e.tensor_copy(
                                qpad_all[rr:rr + 64, 8 * hl:8 * hl + 8],
                                qb_sb[rr:rr + 64, ch * 8:ch * 8 + 8])
                            last = e.tensor_copy(
                                kpad_all[rr:rr + 64, 32 * hl:32 * hl + 32],
                                kb_sb[rr:rr + 64, ch * M_BLK:(ch + 1) * M_BLK])
                        return last
                    V(padfill)
                    vv2 = p.inc("vector", sVE)
                    p.wait("tensor", sVE, vv2)
                    p.wait_res("tensor", "psG")
                    for hl in range(8):
                        T(lambda e, hl=hl:
                          e.matmul(psG[0:8, 32 * hl:32 * hl + 32],
                                   qpad_all[:, 8 * hl:8 * hl + 8],
                                   kpad_all[:, 32 * hl:32 * hl + 32],
                                   start=True, stop=True))
                    vmm = p.inc("tensor", sMM)
                    p.wait("vector", sMM, vmm)
                    def mbatch(e, qr=qr):
                        e.tensor_copy(c_all[0:8, :], psG[0:8, 0:256])
                        e.tensor_reduce(m1a[0:8, :], d3(c_all[0:8, :]),
                                        AX.X, ALU.max)
                        e.tensor_tensor(d3(ism[0:8, :]), d3(c_all[0:8, :]),
                                        m1a[0:8, :].unsqueeze(2)
                                        .broadcast_to([8, 8, M_BLK]),
                                        ALU.is_ge)
                        e.tensor_scalar(ism[0:8, :], ism[0:8, :], -BIG, None,
                                        ALU.mult)
                        e.tensor_add(ism[0:8, :], ism[0:8, :], c_all[0:8, :])
                        e.tensor_reduce(m2a[0:8, :], d3(ism[0:8, :]),
                                        AX.X, ALU.max)
                        e.tensor_tensor(d3(ism[0:8, :]), d3(c_all[0:8, :]),
                                        m2a[0:8, :].unsqueeze(2)
                                        .broadcast_to([8, 8, M_BLK]),
                                        ALU.is_ge)
                        e.tensor_tensor(d3(ism[0:8, :]), d3(ism[0:8, :]),
                                        cst_eye32[0:8, ts(qr, M_BLK)]
                                        .unsqueeze(1)
                                        .broadcast_to([8, 8, M_BLK]),
                                        ALU.max)
                        return e.tensor_scalar(A_all[0:8, :], ism[0:8, :],
                                               1.0, BIG, ALU.subtract,
                                               ALU.mult)
                    V(mbatch)
                    vv2 = p.inc("vector", sVE)
                    p.free_res("psG", sVE, vv2)
                    p.wait("tensor", sVE, vv2)
                    p.wait("gpsimd", sVE, vv2)
                    dbg_dump("mskb0", A_all[0:8, :], 8, 8 * M_BLK)

                    for hl in range(8):
                        ch = hl // 2
                        rr = (hl % 2) * 64
                        p.wait_res("tensor", "psH")
                        T(lambda e, hl=hl:
                          e.transpose(psH[0:M_BLK, 0:8],
                                      A_all[0:8, 32 * hl:32 * hl + 32],
                                      cst_eye[0:8, 0:8]))
                        vmm = p.inc("tensor", sMM)
                        p.wait("vector", sMM, vmm)
                        V(lambda e: e.tensor_copy(mskT[0:M_BLK, 0:8],
                                                  psH[0:M_BLK, 0:8]))
                        vv2 = p.inc("vector", sVE)
                        p.free_res("psH", sVE, vv2)
                        p.wait("tensor", sVE, vv2)
                        for pc in range(4):
                            p.wait_res("tensor", "psH")
                            T(lambda e, pc=pc:
                              e.matmul(psH[0:8, 0:512], mskT[0:M_BLK, 0:8],
                                       cst_rkT[:, ts(pc, 512)],
                                       start=True, stop=True))
                            vmm = p.inc("tensor", sMM)
                            p.wait("scalar", sMM, vmm)
                            SC(lambda e, pc=pc:
                               e.copy(t1sb[0:8, ts(pc, 512)], psH[0:8, 0:512]))
                            vsc = p.inc("scalar", sSC)
                            p.free_res("psH", sSC, vsc)
                        p.wait("tensor", sSC, sSC.n)

                        p.wait_res("tensor", "psF")
                        for qt in range(4):
                            qoff = ch * TT + qt * 128
                            for pc in range(4):
                                pk = ["psA", "psB", "psC", "psD"][pc]
                                pt_ = PS[pk]
                                p.wait_res("tensor", pk)
                                koff = ch * S + pc * 512
                                mm3(pt_[:, 0:512],
                                    qhi[rr:rr + 64, qoff:qoff + 128],
                                    qlo[rr:rr + 64, qoff:qoff + 128],
                                    khi[rr:rr + 64, koff:koff + 512],
                                    klo[rr:rr + 64, koff:koff + 512],
                                    start=True, stop=False, tp=(rr, 0))
                                T(lambda e, pc=pc, qt=qt, pt_=pt_:
                                  e.matmul(pt_[:, 0:512],
                                           cst_sel8[:, ts(qt, 128)],
                                           t1sb[0:8, ts(pc, 512)],
                                           start=False, stop=True))
                                vmm = p.inc("tensor", sMM)
                                p.wait("vector", sMM, vmm)
                                def nred(e, pc=pc, pt_=pt_):
                                    e.tensor_reduce(
                                        statcol[:, 8 * pc:8 * pc + 1],
                                        pt_[:, 0:512], AX.X, ALU.max,
                                        negate=True)
                                    return e.memset(
                                        statcol[:, 8 * pc + 1:8 * pc + 8],
                                        3.0e38)
                                V(nred)
                            def ncomb(e):
                                e.tensor_reduce(statcol[:, 32:33],
                                                statcol[:, 0:32], AX.X,
                                                ALU.min)
                                return e.memset(statcol[:, 33:40], 0.0)
                            V(ncomb)
                            vv2 = p.inc("vector", sVE)
                            p.wait("scalar", sVE, vv2)
                            for pc in range(4):
                                pk = ["psA", "psB", "psC", "psD"][pc]
                                pt_ = PS[pk]
                                SC(lambda e, pc=pc, pt_=pt_:
                                   e.activation(pbuf[:, ts(pc, 512)],
                                                pt_[:, 0:512], AF.Exp,
                                                bias=statcol[:, 32:33],
                                                accum_out=colC[:, 8 * pc:
                                                               8 * pc + 1]))
                                vsc = p.inc("scalar", sSC)
                                p.free_res(pk, sSC, vsc)
                            def cflush(e):
                                last = None
                                for pc in range(4):
                                    last = e.memzero(
                                        colC[:, 8 * pc + 1:8 * pc + 8])
                                return last
                            SC(cflush)
                            p.inc("scalar", sSC)
                            p.wait("vector", sSC, sSC.n)
                            p.wait("gpsimd", sSC, sSC.n)
                            dbg_dump("praw0", pbuf[:, :], 128, S)
                            dbg_dump("colC0", colC[:, 0:4], 128, 4)
                            def pn(e):
                                e.tensor_reduce(statcol[:, 40:41],
                                                colC[:, 0:32], AX.X, ALU.add)
                                e.memset(statcol[:, 41:48], 0.0)
                                e.memset(tmp2[:, 0:TT], 0.0)
                                e.reciprocal(statcol[:, 48:49],
                                             statcol[:, 40:41])
                                e.memset(statcol[:, 49:56], 0.0)
                                return e.memset(tmp2[:, 0:TT], 0.0)
                            V(pn)
                            vv2 = p.inc("vector", sVE)
                            p.wait("scalar", sVE, vv2)
                            p.wait("gpsimd", sVE, vv2)
                            dbg_dump("ncol0", statcol[:, 0:56], 128, 56)
                            SC(lambda e: e.mul(pnorm[:, :], pbuf[:, :],
                                               statcol[:, 48:49]))
                            vsc = p.inc("scalar", sSC)
                            p.wait("tensor", sSC, vsc)
                            p.wait("gpsimd", sSC, vsc)
                            dbg_dump("pbuf0", pnorm[:, :], 128, S)
                            for kb4 in range(4):
                                pk = "psE" if kb4 % 2 == 0 else "psG"
                                pt_ = PS[pk]
                                p.wait_res("tensor", pk)
                                for j in range(4):
                                    kc = kb4 * 4 + j
                                    T(lambda e, kc=kc, j=j, pt_=pt_:
                                      e.transpose(pt_[:, ts(j, 128)],
                                                  pnorm[:, ts(kc, 128)],
                                                  cst_eye[:, :]))
                                vmm = p.inc("tensor", sMM)
                                p.wait("vector", sMM, vmm)
                                V(lambda e, kb4=kb4, pt_=pt_:
                                  e.tensor_copy(ptb[:, ts(kb4, 512)],
                                                pt_[:, 0:512]))
                                vev = p.inc("vector", sEV)
                                p.free_res(pk, sEV, vev)
                                p.wait("tensor", sEV, vev)
                                if kb4 == 0:
                                    p.wait("gpsimd", sEV, vev)
                                    dbg_dump("ptb0", ptb[:, 0:512], 128, 512)
                                for j in range(4):
                                    kc = kb4 * 4 + j
                                    T(lambda e, kc=kc, qt=qt, hl=hl:
                                      e.matmul(psF[0:64, ts(qt, 128)],
                                               vtm[:, kc * 512 + hl * 64:
                                                   kc * 512 + hl * 64 + 64],
                                               ptb[:, ts(kc, 128)],
                                               start=(kc == 0),
                                               stop=(kc == 15)))
                            vmm = p.inc("tensor", sMM)
                            p.wait("vector", sMM, vmm)
                        p.wait_res("vector", "stg")
                        def oev(e):
                            e.tensor_copy(stghi[0:64, 0:TT], psF[0:64, 0:512])
                            return e.tensor_tensor(stglo[0:64, 0:TT],
                                                   psF[0:64, 0:512],
                                                   stghi[0:64, 0:TT],
                                                   ALU.subtract)
                        V(oev)
                        vev = p.inc("vector", sEV)
                        p.free_res("psF", sEV, vev)
                        p.wait("gpsimd", sEV, vev)
                        dbg_dump("stg0", stghi[0:64, 0:TT], 64, TT)
                        chunk_g = (hg * 512 + hl * 64) // 128
                        rr2 = (hl % 2) * 64
                        qX.dma(ot_hi[chunk_g, rr2:rr2 + 64, ts(qr, TT)],
                               stghi[0:64, 0:TT])
                        vst = qX.dma(ot_lo[chunk_g, rr2:rr2 + 64, ts(qr, TT)],
                                     stglo[0:64, 0:TT])
                        p.free_res("stg", qX.sem, vst)
                p.free_res("wslab", sMM, sMM.n)
            p.barrier(dma_queues=(qX, qW))

            AR2 = Arena()
            pwhi = AR2.allocb(8 * D)
            pwlo = AR2.allocb(8 * D)
            othi = AR2.allocb(8 * TT)
            otlo = AR2.allocb(8 * TT)
            xo_p = AR2.alloc(8 * TT)
            split_weightT(proj_wT, pwhi, pwlo, n_i=8, o_full=D)
            p.wait("tensor", sVE, sVE.n)
            for tt in range(4):
                for i in range(8):
                    qX.dma(othi[:, ts(i, TT)], ot_hi[i, :, ts(tt, TT)])
                    vot = qX.dma(otlo[:, ts(i, TT)], ot_lo[i, :, ts(tt, TT)])
                vx = load_x_tile(x_in, seq, tt * TT)
                p.wait("tensor", qX.sem, vot)
                p.wait("vector", qX.sem, vx)
                for oc in range(8):
                    pk = "psA" if oc % 2 == 0 else "psB"
                    pt_ = PS[pk]
                    p.wait_res("tensor", pk)
                    for i in range(8):
                        mm3(pt_[:, 0:TT],
                            pwhi[:, i * D + oc * 128:i * D + oc * 128 + 128],
                            pwlo[:, i * D + oc * 128:i * D + oc * 128 + 128],
                            othi[:, ts(i, TT)], otlo[:, ts(i, TT)],
                            start=(i == 0), stop=(i == 7))
                    vmm = p.inc("tensor", sMM)
                    p.wait("scalar", sMM, vmm)
                    p.wait_res("scalar", "tmp1p")
                    SC(lambda e, oc=oc, pt_=pt_:
                       e.add(tmp1[:, 0:TT], pt_[:, 0:TT], sc_pb[:, oc:oc + 1]))
                    vsc = p.inc("scalar", sSC)
                    p.free_res(pk, sSC, vsc)
                    p.wait("vector", sSC, vsc)
                    p.wait_res("vector", "xo_p")
                    V(lambda e, oc=oc:
                      e.tensor_add(xo_p[:, ts(oc, TT)], tmp1[:, 0:TT],
                                   x_sb[:, ts(oc, TT)]))
                    vv2 = p.inc("vector", sVE)
                    p.free_res("tmp1p", sVE, vv2)
                p.wait("gpsimd", sVE, sVE.n)
                store_x_tile(x1, seq, tt * TT, xo_p, "xo_p")
                p.free_res("x_sb", qX.sem, qX.sem.n)
            p.barrier(dma_queues=(qX, qW))

    def ffn(x_src, x_dst):
        AR = Arena()
        hh = AR.allocb(32 * TT)
        hl_ = AR.allocb(32 * TT)
        h2h_off = AR.off
        h2h = AR.allocb(32 * TT)
        h2l = AR.allocb(32 * TT)
        xnh = arena[:, h2h_off:h2h_off + 2048].bitcast(BF16)
        xnl = arena[:, h2h_off + 2048:h2h_off + 4096].bitcast(BF16)
        xo_f = arena[:, h2h_off:h2h_off + 4096]

        for seq in range(B_CORE):
            for tt in range(4):
                vx = load_x_tile(x_src, seq, tt * TT)
                vln = ln_tile(vx)
                vn = normalize_split(vln, xnh, xnl)
                p.wait("tensor", sVE, vn)

                for oc in range(32):
                    wb = wbufA if oc % 2 == 0 else wbufB
                    wkey = "wA" if oc % 2 == 0 else "wB"
                    p.wait_res("sync", wkey)
                    qW.dma(wb[:, 0:1024], c_w1[oc].rearrange("p a b -> p (a b)"))
                    v2 = qW.dma(wb[:, 1024:2048],
                                c_w2[oc].rearrange("p a b -> p (a b)"))
                    k1 = "psA" if oc % 2 == 0 else "psC"
                    k2 = "psB" if oc % 2 == 0 else "psD"
                    pt1 = PS[k1]; pt2 = PS[k2]
                    p.wait("tensor", qW.sem, v2)
                    p.wait_res("tensor", k1)
                    p.wait_res("tensor", k2)
                    for i in range(8):
                        mm2(pt1[:, 0:TT], wb[:, ts(i, 128)],
                            xnh[:, ts(i, TT)], xnl[:, ts(i, TT)],
                            start=(i == 0), stop=(i == 7))
                    for i in range(8):
                        mm2(pt2[:, 0:TT], wb[:, 1024 + i * 128:1152 + i * 128],
                            xnh[:, ts(i, TT)], xnl[:, ts(i, TT)],
                            start=(i == 0), stop=(i == 7))
                    vmm = p.inc("tensor", sMM)
                    p.free_res(wkey, sMM, vmm)
                    p.wait("scalar", sMM, vmm)
                    p.wait_res("scalar", "tmps")
                    SC(lambda e, oc=oc, pt1=pt1:
                       e.activation(tmp1[:, 0:TT], pt1[:, 0:TT], AF.Silu,
                                    scale=sc_s1[:, oc:oc + 1]))
                    SC(lambda e, oc=oc, pt2=pt2:
                       e.mul(tmp2[:, 0:TT], pt2[:, 0:TT], sc_s2[:, oc:oc + 1]))
                    vsc = p.inc("scalar", sSC)
                    p.free_res(k1, sSC, vsc)
                    p.free_res(k2, sSC, vsc)
                    p.wait("vector", sSC, vsc)
                    def hev(e, oc=oc):
                        e.tensor_mul(tmp3[:, 0:TT], tmp1[:, 0:TT], tmp2[:, 0:TT])
                        e.tensor_copy(hh[:, ts(oc, TT)], tmp3[:, 0:TT])
                        return e.tensor_tensor(hl_[:, ts(oc, TT)], tmp3[:, 0:TT],
                                               hh[:, ts(oc, TT)], ALU.subtract)
                    V(hev)
                    vv = p.inc("vector", sVE)
                    p.free_res("tmps", sVE, vv)

                def big_layer(cache, scol, src_h, src_l, dst_h, dst_l, n_oc,
                              is_w3=False):
                    p.wait("tensor", sVE, sVE.n)
                    for oc in range(n_oc):
                        wb = wbufA if oc % 2 == 0 else wbufB
                        wkey = "wA" if oc % 2 == 0 else "wB"
                        p.wait_res("sync", wkey)
                        vw = qW.dma(wb[:, 0:4096],
                                    cache[oc].rearrange("p a b -> p (a b)"))
                        pk = "psA" if oc % 2 == 0 else "psB"
                        pt_ = PS[pk]
                        p.wait("tensor", qW.sem, vw)
                        p.wait_res("tensor", pk)
                        for i in range(32):
                            mm2(pt_[:, 0:TT], wb[:, ts(i, 128)],
                                src_h[:, ts(i, TT)], src_l[:, ts(i, TT)],
                                start=(i == 0), stop=(i == 31))
                        vmm = p.inc("tensor", sMM)
                        p.free_res(wkey, sMM, vmm)
                        p.wait("scalar", sMM, vmm)
                        p.wait_res("scalar", "tmps")
                        if is_w3:
                            SC(lambda e, oc=oc, pt_=pt_:
                               e.mul(tmp1[:, 0:TT], pt_[:, 0:TT],
                                     scol[:, oc:oc + 1]))
                        else:
                            SC(lambda e, oc=oc, pt_=pt_, scol=scol:
                               e.activation(tmp1[:, 0:TT], pt_[:, 0:TT],
                                            AF.Gelu, scale=scol[:, oc:oc + 1]))
                        vsc = p.inc("scalar", sSC)
                        p.free_res(pk, sSC, vsc)
                        p.wait("vector", sSC, vsc)
                        if is_w3:
                            V(lambda e, oc=oc:
                              e.tensor_add(xo_f[:, ts(oc, TT)], tmp1[:, 0:TT],
                                           x_sb[:, ts(oc, TT)]))
                        else:
                            def gev(e, oc=oc, dst_h=dst_h, dst_l=dst_l):
                                e.tensor_copy(dst_h[:, ts(oc, TT)],
                                              tmp1[:, 0:TT])
                                return e.tensor_tensor(dst_l[:, ts(oc, TT)],
                                                       tmp1[:, 0:TT],
                                                       dst_h[:, ts(oc, TT)],
                                                       ALU.subtract)
                            V(gev)
                        vv = p.inc("vector", sVE)
                        p.free_res("tmps", sVE, vv)

                big_layer(c_h1, sc_sh1, hh, hl_, h2h, h2l, 32)
                big_layer(c_h2, sc_sh2, h2h, h2l, hh, hl_, 32)
                big_layer(c_w3, sc_s3, hh, hl_, None, None, 8, is_w3=True)
                p.wait("gpsimd", sVE, sVE.n)
                store_x_tile(x_dst, seq, tt * TT, xo_f, "xo_f")
                p.free_res("x_sb", qX.sem, qX.sem.n)
        p.barrier(dma_queues=(qX, qW))

    def attn_b():
        AR = Arena()
        xnh = AR.allocb(8 * TT)
        xnl = AR.allocb(8 * TT)
        wh1 = AR.allocb(8 * D)
        wl1 = AR.allocb(8 * D)
        wh2 = AR.allocb(8 * D)
        wl2 = AR.allocb(8 * D)
        ktm_off = AR.off
        ktm = AR.alloc(4 * D)
        vtm2_off = AR.off
        vtm2 = AR.alloc(4 * D)
        xo_b = arena[:, vtm2_off:vtm2_off + 4096]
        kv2 = AR.alloc(D)
        ksum3 = AR.alloc(128)
        den = AR.alloc(TT)
        ksrow = AR.alloc(2 * TT)
        qf = arena[:, ktm_off:ktm_off + 8 * TT]

        for seq in range(B_CORE):
            split_weightT(wkT, wh1, wl1)
            split_weightT(wvT, wh2, wl2)
            p.wait("tensor", sVE, sVE.n)
            V(lambda e: e.memset(kv2[:], 0.0))
            V(lambda e: e.memset(ksum3[:], 0.0))
            V(lambda e: e.memset(den[:], 0.0))
            V(lambda e: e.memset(ksrow[0:1, :], 0.0))
            vv = p.inc("vector", sVE)
            p.wait("tensor", sVE, vv)
            p.wait("gpsimd", sVE, vv)
            p.wait_res("tensor", "psE")
            p.wait_res("tensor", "psF")
            for tt in range(4):
                vx = load_x_tile(x2, seq, tt * TT)
                vln = ln_tile(vx)
                vn = normalize_split(vln, xnh, xnl)
                p.wait("tensor", sVE, vn)
                for which in range(2):
                    whi_ = wh1 if which == 0 else wh2
                    wlo_ = wl1 if which == 0 else wl2
                    for t128 in range(4):
                        for ep in range(2):
                            pk = "psG" if (t128 * 2 + ep) % 2 == 0 else "psH"
                            pt_ = PS[pk]
                            p.wait_res("tensor", pk)
                            for i in range(8):
                                eoff = i * D + ep * 512
                                a_sl = slice(i * TT + t128 * 128,
                                             i * TT + t128 * 128 + 128)
                                T(lambda e, pt_=pt_, whi_=whi_, eoff=eoff,
                                  a_sl=a_sl, i=i:
                                  e.matmul(pt_[:, 0:512], xnh[:, a_sl],
                                           whi_[:, eoff:eoff + 512],
                                           start=(i == 0), stop=False))
                                T(lambda e, pt_=pt_, wlo_=wlo_, eoff=eoff,
                                  a_sl=a_sl:
                                  e.matmul(pt_[:, 0:512], xnh[:, a_sl],
                                           wlo_[:, eoff:eoff + 512],
                                           start=False, stop=False))
                                T(lambda e, pt_=pt_, whi_=whi_, eoff=eoff,
                                  a_sl=a_sl, i=i:
                                  e.matmul(pt_[:, 0:512], xnl[:, a_sl],
                                           whi_[:, eoff:eoff + 512],
                                           start=False, stop=(i == 7)))
                            vmm = p.inc("tensor", sMM)
                            doff = t128 * D + ep * 512
                            p.wait("vector", sMM, vmm)
                            if which == 0:
                                p.wait_res("vector", "tmpk")
                                V(lambda e, pt_=pt_:
                                  e.tensor_scalar_min(tmp1[:, 0:TT],
                                                      pt_[:, 0:512], 0.0))
                                vv = p.inc("vector", sVE)
                                p.wait("scalar", sVE, vv)
                                SC(lambda e: e.activation(tmp2[:, 0:TT],
                                                          tmp1[:, 0:TT],
                                                          AF.Exp))
                                vsc = p.inc("scalar", sSC)
                                p.wait("vector", sSC, vsc)
                                V(lambda e, pt_=pt_, doff=doff:
                                  e.scalar_tensor_tensor(
                                      ktm[:, doff:doff + 512], pt_[:, 0:512],
                                      0.0, tmp2[:, 0:TT], ALU.max, ALU.add))
                                vv = p.inc("vector", sVE)
                                p.free_res(pk, sVE, vv)
                                p.free_res("tmpk", sVE, vv)
                            else:
                                V(lambda e, pt_=pt_, doff=doff:
                                  e.tensor_copy(vtm2[:, doff:doff + 512],
                                                pt_[:, 0:512]))
                                vv = p.inc("vector", sVE)
                                p.free_res(pk, sVE, vv)
                p.wait("tensor", sVE, sVE.n)
                p.wait("gpsimd", sVE, sVE.n)
                dbg_dump("ktm0", ktm[:, 0:TT], 128, TT)
                dbg_dump("vtm0b", vtm2[:, 0:TT], 128, TT)
                for hl in range(16):
                    pt_ = psE if hl < 8 else psF
                    rr = (hl % 2) * 64
                    cofs = (hl % 8) * 64
                    for tc in range(4):
                        T(lambda e, hl=hl, tc=tc, pt_=pt_, rr=rr, cofs=cofs,
                          tt=tt:
                          e.matmul(pt_[rr:rr + 64, cofs:cofs + 64],
                                   ktm[:, tc * D + hl * 64:
                                       tc * D + hl * 64 + 64],
                                   vtm2[:, tc * D + hl * 64:
                                        tc * D + hl * 64 + 64],
                                   start=(tt == 0 and tc == 0),
                                   stop=(tt == 3 and tc == 3),
                                   tile_position=(0, rr)))
                p.wait_res("tensor", "psG")
                for ep in range(2):
                    for tc in range(4):
                        T(lambda e, ep=ep, tc=tc:
                          e.matmul(psG[0:1, 0:512], cst_ones[:, :],
                                   ktm[:, tc * D + ep * 512:
                                       tc * D + ep * 512 + 512],
                                   start=(tc == 0), stop=(tc == 3)))
                    vmm = p.inc("tensor", sMM)
                    p.wait("vector", sMM, vmm)
                    V(lambda e, ep=ep:
                      e.tensor_add(ksrow[0:1, ts(ep, TT)],
                                   ksrow[0:1, ts(ep, TT)], psG[0:1, 0:512]))
                    vv = p.inc("vector", sVE)
                    p.free_res("psG", sVE, vv)
                    p.wait("tensor", sVE, vv)
            vmm = p.inc("tensor", sMM)
            p.wait("vector", sMM, vmm)
            def kvev(e):
                last = None
                for hl in range(16):
                    pt_ = psE if hl < 8 else psF
                    rr = (hl % 2) * 64
                    cofs = (hl % 8) * 64
                    last = e.tensor_copy(kv2[rr:rr + 64, hl * 64:hl * 64 + 64],
                                         pt_[rr:rr + 64, cofs:cofs + 64])
                return last
            V(kvev)
            vv = p.inc("vector", sVE)
            p.free_res("psE", sVE, vv)
            p.free_res("psF", sVE, vv)
            p.wait("gpsimd", sVE, vv)
            for h2_ in range(16):
                ep = h2_ // 8
                src_off = ep * TT + (h2_ % 8) * 64
                rr = (h2_ % 2) * 64
                c = h2_ // 2
                qX.dma(ksum3[rr:rr + 64, c * 16 + h2_:c * 16 + h2_ + 1],
                       ksrow[0:1, src_off:src_off + 64])
            vks = qX.sem.n
            p.wait("tensor", qX.sem, vks)
            p.wait("vector", qX.sem, vks)
            dbg_dump("kv20", kv2[:, :], 128, D)
            dbg_dump("ksr0", ksrow[0:1, :], 1, 2 * TT)
            dbg_dump("ks30", ksum3[:, :], 128, 128)

            split_weightT(wqT, wh1, wl1)
            split_weightT(woT, wh2, wl2)
            p.wait("tensor", sVE, sVE.n)
            for tt in range(4):
                vx = load_x_tile(x2, seq, tt * TT)
                vln = ln_tile(vx)
                vn = normalize_split(vln, xnh, xnl)
                p.wait("tensor", sVE, vn)
                for oc in range(8):
                    pk = "psA" if oc % 2 == 0 else "psB"
                    pt_ = PS[pk]
                    p.wait_res("tensor", pk)
                    for i in range(8):
                        mm3(pt_[:, 0:TT],
                            wh1[:, i * D + oc * 128:i * D + oc * 128 + 128],
                            wl1[:, i * D + oc * 128:i * D + oc * 128 + 128],
                            xnh[:, ts(i, TT)], xnl[:, ts(i, TT)],
                            start=(i == 0), stop=(i == 7))
                    vmm = p.inc("tensor", sMM)
                    p.wait("vector", sMM, vmm)
                    p.wait_res("vector", "tmpk")
                    V(lambda e, pt_=pt_: e.tensor_scalar_min(tmp1[:, 0:TT],
                                                             pt_[:, 0:TT], 0.0))
                    vv = p.inc("vector", sVE)
                    p.wait("scalar", sVE, vv)
                    SC(lambda e: e.activation(tmp2[:, 0:TT], tmp1[:, 0:TT],
                                              AF.Exp))
                    vsc = p.inc("scalar", sSC)
                    p.wait("vector", sSC, vsc)
                    V(lambda e, oc=oc, pt_=pt_:
                      e.scalar_tensor_tensor(qf[:, ts(oc, TT)], pt_[:, 0:TT],
                                             0.0, tmp2[:, 0:TT], ALU.max,
                                             ALU.add))
                    vv = p.inc("vector", sVE)
                    p.free_res(pk, sVE, vv)
                    p.free_res("tmpk", sVE, vv)
                p.wait("tensor", sVE, sVE.n)
                p.wait_res("tensor", "psG")
                for c in range(8):
                    T(lambda e, c=c:
                      e.matmul(psG[0:16, 0:TT], ksum3[:, c * 16:c * 16 + 16],
                               qf[:, ts(c, TT)], start=(c == 0), stop=(c == 7)))
                vmm = p.inc("tensor", sMM)
                p.wait("vector", sMM, vmm)
                def recp(e):
                    e.tensor_copy(den[0:16, 0:TT], psG[0:16, 0:TT])
                    e.tensor_scalar_add(den[0:16, 0:TT], den[0:16, 0:TT],
                                        ATTB_EPS)
                    return e.reciprocal(den[0:16, 0:TT], den[0:16, 0:TT])
                V(recp)
                vv = p.inc("vector", sVE)
                p.free_res("psG", sVE, vv)
                p.wait("tensor", sVE, vv)
                p.wait("gpsimd", sVE, vv)
                dbg_dump("den0", den[0:16, :], 16, TT)
                dbg_dump("qf0", qf[:, 0:TT], 128, TT)
                for c in range(8):
                    p.wait_res("tensor", "psE")
                    T(lambda e, c=c:
                      e.matmul(psE[0:64, 0:TT],
                               kv2[:, (2 * c) * 64:(2 * c) * 64 + 64],
                               qf[:, ts(c, TT)], start=True, stop=True,
                               tile_position=(0, 0)))
                    T(lambda e, c=c:
                      e.matmul(psE[64:128, 0:TT],
                               kv2[:, (2 * c + 1) * 64:(2 * c + 1) * 64 + 64],
                               qf[:, ts(c, TT)], start=True, stop=True,
                               tile_position=(0, 64)))
                    p.wait_res("tensor", "psH")
                    T(lambda e, c=c:
                      e.matmul(psH[:, 0:TT], cst_selh[:, ts(c, 128)],
                               den[:, 0:TT], start=True, stop=True))
                    vmm = p.inc("tensor", sMM)
                    p.wait("vector", sMM, vmm)
                    p.wait_res("vector", "tmpk")
                    def onrm(e, c=c):
                        e.tensor_copy(tmp1[:, 0:TT], psH[:, 0:TT])
                        e.tensor_tensor(tmp2[:, 0:TT], psE[:, 0:TT],
                                        tmp1[:, 0:TT], ALU.mult)
                        e.tensor_copy(xnh[:, ts(c, TT)], tmp2[:, 0:TT])
                        return e.tensor_tensor(xnl[:, ts(c, TT)], tmp2[:, 0:TT],
                                               xnh[:, ts(c, TT)], ALU.subtract)
                    V(onrm)
                    vv = p.inc("vector", sVE)
                    p.free_res("psE", sVE, vv)
                    p.free_res("psH", sVE, vv)
                    p.free_res("tmpk", sVE, vv)
                p.wait("tensor", sVE, sVE.n)
                for oc in range(8):
                    pk = "psA" if oc % 2 == 0 else "psB"
                    pt_ = PS[pk]
                    p.wait_res("tensor", pk)
                    for i in range(8):
                        mm3(pt_[:, 0:TT],
                            wh2[:, i * D + oc * 128:i * D + oc * 128 + 128],
                            wl2[:, i * D + oc * 128:i * D + oc * 128 + 128],
                            xnh[:, ts(i, TT)], xnl[:, ts(i, TT)],
                            start=(i == 0), stop=(i == 7))
                    vmm = p.inc("tensor", sMM)
                    p.wait("scalar", sMM, vmm)
                    p.wait_res("scalar", "tmp1p")
                    SC(lambda e, oc=oc, pt_=pt_:
                       e.add(tmp1[:, 0:TT], pt_[:, 0:TT], sc_bo[:, oc:oc + 1]))
                    vsc = p.inc("scalar", sSC)
                    p.free_res(pk, sSC, vsc)
                    p.wait("vector", sSC, vsc)
                    p.wait_res("vector", "xo_b")
                    V(lambda e, oc=oc:
                      e.tensor_add(xo_b[:, ts(oc, TT)], tmp1[:, 0:TT],
                                   x_sb[:, ts(oc, TT)]))
                    vv = p.inc("vector", sVE)
                    p.free_res("tmp1p", sVE, vv)
                p.wait("gpsimd", sVE, sVE.n)
                store_x_tile(x3, seq, tt * TT, xo_b, "xo_b")
                p.free_res("x_sb", qX.sem, qX.sem.n)
            p.barrier(dma_queues=(qX, qW))

    attn_a()
    if _DEBUG_STOP_AFTER != "attn_a":
        ffn(x1, x2)
        if _DEBUG_STOP_AFTER != "ffn1":
            attn_b()
            if _DEBUG_STOP_AFTER != "attn_b":
                ffn(x3, x_out)
    p.barrier(dma_queues=(qX, qW))

    with nc.Block() as block:
        p.emit(block)
    return nc


_NC_CACHE = {}

def build_inputs(inputs):
    def bf(a):
        return np.ascontiguousarray(a)

    sh = dict(
        qkv_wT=bf(np.asarray(inputs["L0_attn_a_qkv_w"]).T),
        proj_wT=bf(np.asarray(inputs["L0_attn_a_proj_w"]).T),
        proj_b2=bf(np.asarray(inputs["L0_attn_a_proj_b"]).reshape(8, 128).T),
        wqT=bf(np.asarray(inputs["L0_attn_b_wq"]).T),
        wkT=bf(np.asarray(inputs["L0_attn_b_wk"]).T),
        wvT=bf(np.asarray(inputs["L0_attn_b_wv"]).T),
        woT=bf(np.asarray(inputs["L0_attn_b_wo"]).T),
        bo2=bf(np.asarray(inputs["L0_attn_b_bo"]).reshape(8, 128).T),
        w1T=bf(np.asarray(inputs["L0_ffn_w1_w"]).T),
        w2T=bf(np.asarray(inputs["L0_ffn_w2_w"]).T),
        w3T=bf(np.asarray(inputs["L0_ffn_w3_w"]).T),
        h1T=bf(np.asarray(inputs["L0_ffn_h1_w"]).T),
        h2T=bf(np.asarray(inputs["L0_ffn_h2_w"]).T),
        s1c=bf(np.asarray(inputs["L0_ffn_w1_s"]).reshape(32, 128).T),
        s2c=bf(np.asarray(inputs["L0_ffn_w2_s"]).reshape(32, 128).T),
        sh1c=bf(np.asarray(inputs["L0_ffn_h1_s"]).reshape(32, 128).T),
        sh2c=bf(np.asarray(inputs["L0_ffn_h2_s"]).reshape(32, 128).T),
        s3c=bf(np.asarray(inputs["L0_ffn_w3_s"]).reshape(8, 128).T),
    )
    e1 = np.zeros((128, 128), np.float32); e1[0, :] = 1.0
    sh["e1c"] = e1
    sh["eye128c"] = np.eye(128, dtype=np.float32)
    sh["ones_c"] = np.ones((128, 1), np.float32)
    rkT = np.zeros((M_BLK, S), ml_dtypes.bfloat16)
    for n in range(M_BLK):
        rkT[n, n * BS:(n + 1) * BS] = 1
    sh["rkTc"] = rkT
    sel8 = np.zeros((8, 512), ml_dtypes.bfloat16)
    for qt in range(4):
        for q in range(128):
            sel8[2 * qt + q // 64, qt * 128 + q] = 1
    sh["sel8c"] = sel8
    eye32 = np.zeros((8, 4 * M_BLK), np.float32)
    for qr in range(4):
        for j in range(8):
            eye32[j, qr * M_BLK + qr * 8 + j] = 1.0
    sh["eye32c"] = eye32
    selh = np.zeros((128, D), np.float32)
    for e_ in range(D):
        selh[e_ // 64, e_] = 1.0
    sh["selhc"] = selh
    return sh


def kernel(**inputs):
    if "main" not in _NC_CACHE:
        _NC_CACHE["main"] = build_core_program()
    nc = _NC_CACHE["main"]
    x = np.asarray(inputs["L0_attn_a_x"])
    shared = build_inputs(inputs)
    in_maps = []
    for c in range(8):
        m = dict(shared)
        m["x"] = np.ascontiguousarray(
            x[c * B_CORE:(c + 1) * B_CORE].transpose(0, 2, 1))
        in_maps.append(m)
    res = run_bass_kernel_spmd(nc, in_maps, core_ids=list(range(8)))
    outs = [r["out"].transpose(0, 2, 1) for r in res.results]
    return np.ascontiguousarray(np.concatenate(outs, axis=0))
